# version 01
# baseline (speedup 1.0000x reference)
import numpy as np
import ml_dtypes
import concourse.bass as bass
import concourse.mybir as mybir
from concourse.bass_utils import run_bass_kernel_spmd
from contextlib import ExitStack

F32 = mybir.dt.float32
BF16 = mybir.dt.bfloat16
I32 = mybir.dt.int32
AF = mybir.ActivationFunctionType
ALU = mybir.AluOpType
AX = mybir.AxisListType
BF = ml_dtypes.bfloat16

D = 2048
FI = 8192
HD = 128
SEQ = 4096
TPC = 1024
NT = 8
KD = D // 128
KI = FI // 128
NHC = 4
SCALE = float(HD) ** -0.5
EPS = 1e-5
NEG = -1e9
G4 = [[0, 1, 2, 3], [4, 5, 6, 7]]
G8 = [list(range(8))]


class _S:
    def __init__(self, ctx, nc, name):
        self.h = ctx.enter_context(nc.semaphore(name))
        self.n = 0

    def inc(self, instr, k=1):
        instr.then_inc(self.h, k)
        self.n += k
        return self.n


class DS:

    def __init__(self, ctx, nc, name):
        self.h = ctx.enter_context(nc.semaphore(name))
        self.n = 0

    def go(self, e, out, in_, transpose=False):
        e.wait_ge(self.h, self.n)
        if transpose:
            ins = e.dma_start_transpose(out, in_)
        else:
            ins = e.dma_start(out=out, in_=in_)
        ins.then_inc(self.h, 16)
        self.n += 16
        return self.n


GU_NCH = FI // 512


class V:
    _OPS = {"tensor_scalar", "tensor_tensor", "tensor_copy", "tensor_reduce",
            "scalar_tensor_tensor", "reciprocal", "memset"}

    def __init__(self, e, ch, state):
        self._e = e
        self._ch = ch
        self._st = state
        self._ms = None

    def milestone(self, s, k=1):
        self._ms = (s, k)

    def __getattr__(self, k):
        a = getattr(self._e, k)
        if k in V._OPS:
            def w(*args, **kw):
                if self._st["last"] is not None:
                    h, v = self._st["last"]
                    self._e.wait_ge(h, v)
                ins = a(*args, **kw)
                ms, mk = (self._ms or (self._ch, 1))
                ms.inc(ins, mk)
                self._st["last"] = (ms.h, ms.n)
                self._ms = None
                return ins
            return w
        return a


def _build(stage=3):
    nc = bass.Bass("TRN2", target_bir_lowering=False,
                   detect_race_conditions=False)

    def din(name, shape, dt=F32):
        return nc.dram_tensor(name, shape, dt, kind="ExternalInput").ap()

    x_d = din("x", [TPC, D])
    wq_d = din("wq", [NHC * HD, D])
    wk_d = din("wk", [NHC * HD, D])
    wv_d = din("wv", [NHC * HD, D])
    wo_d = din("wo", [D, NHC * HD])
    wg_d = din("wg", [FI // 8, D])
    wu_d = din("wu", [FI // 8, D])
    wd_d = din("wd", [D // 8, FI])
    cmask_d = din("cmask", [128, 512])
    identb_d = din("identb", [128, 128], BF16)
    identf_d = din("identf", [128, 128])
    out_d = nc.dram_tensor("out", [TPC, D], F32, kind="ExternalOutput").ap()

    wqb = nc.dram_tensor("wqb", [NHC * HD, D], BF16).ap()
    wkb = nc.dram_tensor("wkb", [NHC * HD, D], BF16).ap()
    wvb = nc.dram_tensor("wvb", [NHC * HD, D], BF16).ap()
    wob = nc.dram_tensor("wob", [D, NHC * HD], BF16).ap()
    ws_in = nc.dram_tensor("ws_in", [8, 1], F32).ap()
    ws_out = nc.dram_tensor("ws_out", [8, 1], F32).ap()
    wgT_in = nc.dram_tensor("wgT_in", [D, 1024], BF16).ap()
    wuT_in = nc.dram_tensor("wuT_in", [D, 1024], BF16).ap()
    wdT_in = nc.dram_tensor("wdT_in", [FI, 256], BF16).ap()
    wgT_g = nc.dram_tensor("wgT_g", [8 * D, 1024], BF16).ap()
    wuT_g = nc.dram_tensor("wuT_g", [8 * D, 1024], BF16).ap()
    wdT_g = nc.dram_tensor("wdT_g", [8 * FI, 256], BF16).ap()
    xln2T_in = nc.dram_tensor("xln2T_in", [D, TPC], BF16).ap()
    xln2T_g = nc.dram_tensor("xln2T_g", [4 * D, TPC], BF16).ap()
    attnF_d = nc.dram_tensor("attnF_d", [NHC * HD, SEQ], BF16).ap()
    wo_part = nc.dram_tensor("wo_part", [SEQ, D], BF16).ap()
    wo_red = nc.dram_tensor("wo_red", [TPC, D], BF16).ap()
    hsp = nc.dram_tensor("hsp", [TPC, FI], F32).ap()
    hq_d = nc.dram_tensor("hq", [TPC, FI], BF16).ap()
    xres1 = nc.dram_tensor("xres1", [TPC, D], F32).ap()
    xres2 = nc.dram_tensor("xres2", [TPC, D], F32).ap()

    uid = [0]

    def U(name):
        uid[0] += 1
        return f"{name}_{uid[0]}"

    ctx = ExitStack()
    with ctx:
        def sb(name, shape, dt=F32):
            return ctx.enter_context(nc.sbuf_tensor(name, shape, dt))

        identb = sb("s_identb", [128, 128], BF16)
        identf = sb("s_identf", [128, 128])
        cmask = sb("s_cmask", [128, 512])
        onescol = sb("onescol", [128, 1])
        onesrow = sb("onesrow", [1, 128])
        wsb = sb("wsb", [128, 8])
        invwsb = sb("invwsb", [128, 8])
        b18 = sb("b18", [1, 8])
        s81 = sb("s81", [8, 1])
        wsum = sb("wsum", [128, 4])
        xs1 = sb("xs1", [128, 8])
        cg1 = sb("cg1", [128, 8])
        amaxh = sb("amaxh", [128, 8])
        sh1 = sb("sh1", [128, 8])
        invq1 = sb("invq1", [128, 8])
        shd1 = sb("shd1", [128, 8])
        lnt = sb("lnt", [128, 16])
        epsb = sb("epsb", [128, 1])
        gsel = sb("gsel", [128, 2])
        i32g = ctx.enter_context(nc.sbuf_tensor("i32g", [128, 1], I32))
        fstat = sb("fstat", [128, 8])

        sems = {}

        def S(name):
            if name not in sems:
                sems[name] = _S(ctx, nc, name)
            return sems[name]

        bar = S("bar")
        block = ctx.enter_context(nc.Block())
        ENG = ["sync", "scalar", "vector", "tensor", "gpsimd"]

        def barrier():
            base = bar.n
            bar.n += 5
            for en in ENG:
                @getattr(block, en)
                def _(e, en=en):
                    if en == "sync":
                        for ds in ALL_DS:
                            e.wait_ge(ds.h, ds.n)
                    else:
                        e.drain()
                    e.nop().then_inc(bar.h, 1)
                    e.wait_ge(bar.h, base + 5)

        syA, syB = DS(ctx, nc, "syA"), DS(ctx, nc, "syB")
        scA, scB = DS(ctx, nc, "scA"), DS(ctx, nc, "scB")
        gpA, gpB, gpC = DS(ctx, nc, "gpA"), DS(ctx, nc, "gpB"), DS(ctx, nc, "gpC")
        ALL_DS = [syA, syB, scA, scB, gpA, gpB, gpC]

        vch = S("vch")
        vstate = {"last": None}
        vv = S("vv")
        ss = S("ss")
        pe = S("pe")
        pp = S("pp")
        cc = {k: S("cc_" + k) for k in ["ws", "g", "u", "d", "g1", "g2"]}
        ccr = {k: S("ccr_" + k) for k in ["g", "u", "d", "g1", "g2"]}

        tk_const = {}
        tk_cast = {}

        @block.sync
        def _(e):
            tk_const["identb"] = syA.go(e, identb[:], identb_d)
            tk_const["identf"] = syA.go(e, identf[:], identf_d)
            tk_const["cmask"] = syA.go(e, cmask[:], cmask_d)

        @block.gpsimd
        def _(e):
            e.memset(onescol[:], 1.0)
            e.memset(onesrow[:], 1.0)
            e.memset(epsb[:], EPS)
            pp.inc(e.memset(gsel[:], 0.6), 1)
            tk_cast["wqb"] = gpC.go(e, wqb, wq_d)
            tk_cast["wkb"] = gpC.go(e, wkb, wk_d)
            tk_cast["wvb"] = gpC.go(e, wvb, wv_d)
            tk_cast["wob"] = gpC.go(e, wob, wo_d)

        with ExitStack() as p1:
            wt = [p1.enter_context(nc.sbuf_tensor(U("p1w"), [128, FI], F32)) for i in range(2)]
            junk = p1.enter_context(nc.sbuf_tensor(U("p1junk"), [128, FI], BF16))
            absacc = p1.enter_context(nc.sbuf_tensor(U("absacc"), [128, 24], F32))
            psA = p1.enter_context(nc.psum_tensor(U("p1psA"), [128, 512], F32))

            tiles = [(wg_d, ti, D) for ti in range(8)] + \
                    [(wu_d, ti, D) for ti in range(8)] + \
                    [(wd_d, ti, FI) for ti in range(2)]
            ss0 = ss.n
            tk_tiles = []

            @block.sync
            def _(e):
                for i, (src, ti, w) in enumerate(tiles):
                    if i >= 2:
                        e.wait_ge(ss.h, ss0 + i - 1)
                    ds = syA if i % 2 == 0 else syB
                    tk_tiles.append((ds, ds.go(e, wt[i % 2][:, 0:w],
                                               src[ti * 128:(ti + 1) * 128, :])))

            @block.scalar
            def _(e):
                for i, (src, ti, w) in enumerate(tiles):
                    ds, tk = tk_tiles[i]
                    e.wait_ge(ds.h, tk)
                    ss.inc(e.activation(out=junk[:, 0:w], in_=wt[i % 2][:, 0:w],
                                        func=AF.Abs, accum_out=absacc[:, i:i + 1]), 1)

            @block.vector
            def _(e):
                e = V(e, vch, vstate)
                e.wait_ge(pp.h, 1)
                e.tensor_scalar(out=i32g[:, 0:1], in0=gsel[:, 1:2], scalar1=1.0,
                                scalar2=None, op0=ALU.mult)
                e.tensor_scalar(out=gsel[:, 0:1], in0=i32g[:, 0:1], scalar1=-0.5,
                                scalar2=0.5, op0=ALU.mult, op1=ALU.add)
                e.wait_ge(ss.h, ss0 + len(tiles))
                e.tensor_reduce(out=wsum[:, 0:1], in_=absacc[:, 0:8], axis=AX.X, op=ALU.add)
                e.tensor_reduce(out=wsum[:, 1:2], in_=absacc[:, 8:16], axis=AX.X, op=ALU.add)
                e.tensor_reduce(out=wsum[:, 2:3], in_=absacc[:, 16:18], axis=AX.X, op=ALU.add)
                e.milestone(vv, 1)
                e.memset(wsum[:, 3:4], 0.0)

            @block.tensor
            def _(e):
                e.wait_ge(vv.h, vv.n)
                pe.inc(e.matmul(psA[0:4, 0:1], wsum[:, 0:4], onescol[:], start=True, stop=True), 1)

            @block.vector
            def _(e):
                e = V(e, vch, vstate)
                e.wait_ge(pe.h, pe.n)
                e.memset(s81[:], 0.0)
                e.milestone(vv, 1)
                e.tensor_copy(out=s81[0:4, 0:1], in_=psA[0:4, 0:1])

            tk_b18 = []

            @block.gpsimd
            def _(e):
                e.wait_ge(vv.h, vv.n)
                tk = gpC.go(e, ws_in, s81[:])
                e.wait_ge(gpC.h, tk)
                cc["ws"].inc(e.collective_compute(
                    "AllReduce", ALU.add, replica_groups=G8,
                    ins=[ws_in], outs=[ws_out]), 1)
                e.wait_ge(cc["ws"].h, 1)
                tk_b18.append(gpC.go(e, b18[:], ws_out.rearrange("a b -> b a")))

            @block.vector
            def _(e):
                e = V(e, vch, vstate)
                e.wait_ge(gpC.h, tk_b18[0])
                e.milestone(vv, 1)
                e.tensor_scalar(out=b18[:], in0=b18[:], scalar1=1.0 / (D * FI),
                                scalar2=EPS, op0=ALU.mult, op1=ALU.add)

            @block.tensor
            def _(e):
                e.wait_ge(vv.h, vv.n)
                pe.inc(e.matmul(psA[:, 8:16], onesrow[:], b18[:], start=True, stop=True), 1)

            @block.vector
            def _(e):
                e = V(e, vch, vstate)
                e.wait_ge(pe.h, pe.n)
                e.tensor_copy(out=wsb[:], in_=psA[:, 8:16])
                e.milestone(vv, 1)
                e.reciprocal(out=invwsb[:], in_=wsb[:])
            barrier()

        with ExitStack() as p2:
            wt = [p2.enter_context(nc.sbuf_tensor(U("p2w"), [128, FI], F32)) for i in range(2)]
            sgw = p2.enter_context(nc.sbuf_tensor(U("p2sg"), [128, FI], F32))
            i32b = p2.enter_context(nc.sbuf_tensor(U("p2i32"), [128, FI], I32))
            wqt = p2.enter_context(nc.sbuf_tensor(U("p2wq"), [128, FI], BF16))
            contrib = p2.enter_context(nc.sbuf_tensor(U("p2ct"), [128, 16384], BF16))
            psT = [p2.enter_context(nc.psum_tensor(U("p2ps"), [128, 1024], BF16)) for i in range(2)]

            for (src_d, dst_in, dst_g, col, ntile, wwid, ccname) in [
                (wg_d, wgT_in, wgT_g, 0, 8, D, "g"),
                (wu_d, wuT_in, wuT_g, 1, 8, D, "u"),
                (wd_d, wdT_in, wdT_g, 2, 2, FI, "d"),
            ]:
                kw = wwid // 128
                nb = kw // 8
                width = 1024 if wwid == D else 256
                vv0, pe0, ss0w = vv.n, pe.n, ss.n
                tk_w = []

                def evac_cnt(gb, vv0=vv0, nb=nb):
                    t_, b_ = divmod(gb, nb)
                    return vv0 + t_ * (nb + 1) + 1 + b_ + 1

                @block.sync
                def _(e, src_d=src_d, ntile=ntile, wwid=wwid, vv0=vv0, nb=nb,
                      ss0w=ss0w, tk_w=tk_w):
                    for t in range(ntile):
                        if t >= 2:
                            e.wait_ge(vv.h, vv0 + (t - 1) * (nb + 1) + 1)
                            e.wait_ge(ss.h, ss0w + t - 1)
                        ds = syA if t % 2 == 0 else syB
                        tk_w.append((ds, ds.go(e, wt[t % 2][:, 0:wwid],
                                               src_d[t * 128:(t + 1) * 128, :])))

                @block.scalar
                def _(e, ntile=ntile, wwid=wwid, vv0=vv0, nb=nb, tk_w=tk_w):
                    for t in range(ntile):
                        ds, tk = tk_w[t]
                        e.wait_ge(ds.h, tk)
                        if t >= 1:
                            e.wait_ge(vv.h, vv0 + (t - 1) * (nb + 1) + 1)
                        e.sign(out=sgw[:, 0:wwid], in_=wt[t % 2][:, 0:wwid])
                        ss.inc(e.mul(out=sgw[:, 0:wwid], in_=sgw[:, 0:wwid],
                                     mul=gsel[:, 0:1]), 1)

                @block.vector
                def _(e, ntile=ntile, wwid=wwid, col=col, kw=kw, nb=nb,
                      pe0=pe0, vv0=vv0, ss0w=ss0w, tk_w=tk_w):
                    e = V(e, vch, vstate)
                    for t in range(ntile):
                        ds, tk = tk_w[t]
                        e.wait_ge(ds.h, tk)
                        e.wait_ge(ss.h, ss0w + t + 1)
                        e.scalar_tensor_tensor(out=i32b[:, 0:wwid],
                                               in0=wt[t % 2][:, 0:wwid],
                                               scalar=invwsb[:, col:col + 1],
                                               in1=sgw[:, 0:wwid],
                                               op0=ALU.mult, op1=ALU.add)
                        if t >= 1:
                            e.wait_ge(pe.h, pe0 + nb * t)
                        e.milestone(vv, 1)
                        e.tensor_scalar(out=wqt[:, 0:wwid], in0=i32b[:, 0:wwid],
                                        scalar1=1.0, scalar2=-1.0,
                                        op0=ALU.min, op1=ALU.max)
                        for b in range(nb):
                            e.wait_ge(pe.h, pe0 + nb * t + b + 1)
                            view = contrib.ap().rearrange("p (k m) -> p k m", k=kw)
                            e.milestone(vv, 1)
                            e.tensor_copy(
                                out=view[:, 8 * b:8 * (b + 1), t * 128:(t + 1) * 128],
                                in_=psT[b % 2][:, :])

                @block.tensor
                def _(e, ntile=ntile, kw=kw, nb=nb, vv0=vv0, evac_cnt=evac_cnt):
                    e.wait_ge(syA.h, tk_const["identb"])
                    for t in range(ntile):
                        e.wait_ge(vv.h, vv0 + t * (nb + 1) + 1)
                        for b in range(nb):
                            gb = nb * t + b
                            if gb >= 2:
                                e.wait_ge(vv.h, evac_cnt(gb - 2))
                            for j in range(8):
                                kc = 8 * b + j
                                ins = e.transpose(psT[b % 2][:, j * 128:(j + 1) * 128],
                                                  wqt[:, kc * 128:(kc + 1) * 128], identb[:])
                            pe.inc(ins, 1)
                barrier()

                @block.gpsimd
                def _(e, dst_in=dst_in, dst_g=dst_g, kw=kw, width=width, ccname=ccname):
                    tk = gpC.go(
                        e, dst_in.rearrange("(k p) m -> p k m", k=kw),
                        contrib.ap().rearrange("p (k m) -> p k m", k=kw)[:, :, 0:width])
                    e.wait_ge(gpC.h, tk)
                    cc[ccname].inc(e.collective_compute(
                        "AllGather", ALU.bypass, replica_groups=G8,
                        ins=[dst_in], outs=[dst_g]), 1)
                barrier()

            @block.gpsimd
            def _(e):
                for k in ("g", "u", "d"):
                    e.wait_ge(cc[k].h, 1)
                    ccr[k].inc(e.nop(), 1)

        def emit_ln_quant(src_dram, xqT_t, note):
            with ExitStack() as p3:
                xt = [p3.enter_context(nc.sbuf_tensor(U(f"{note}x"), [128, D], F32)) for i in range(2)]
                junk = p3.enter_context(nc.sbuf_tensor(U(f"{note}jk"), [128, D], BF16))
                xln = p3.enter_context(nc.sbuf_tensor(U(f"{note}ln"), [128, D], F32))
                i32s = p3.enter_context(nc.sbuf_tensor(U(f"{note}i32"), [128, D], I32))
                xq = [p3.enter_context(nc.sbuf_tensor(U(f"{note}q"), [128, D], BF16)) for i in range(2)]
                sgx = p3.enter_context(nc.sbuf_tensor(U(f"{note}sg"), [128, D], F32))
                psT = [p3.enter_context(nc.psum_tensor(U(f"{note}ps"), [128, 1024], BF16)) for i in range(2)]
                vv0, ss0, pe0 = vv.n, ss.n, pe.n
                tk_x = []

                @block.sync
                def _(e):
                    for t in range(NT):
                        if t >= 2:
                            e.wait_ge(ss.h, ss0 + 4 * (t - 1))
                        ds = syA if t % 2 == 0 else syB
                        tk_x.append((ds, ds.go(e, xt[t % 2][:],
                                               src_dram[t * 128:(t + 1) * 128, :])))

                @block.vector
                def _(e):
                    e = V(e, vch, vstate)
                    for t in range(NT):
                        b = 6 * (t % 2)
                        ds, tk = tk_x[t]
                        e.wait_ge(ds.h, tk)
                        e.tensor_reduce(out=lnt[:, b:b + 1], in_=xt[t % 2][:],
                                        axis=AX.X, op=ALU.add)
                        e.wait_ge(ss.h, ss0 + 4 * t + 1)
                        e.tensor_scalar(out=lnt[:, b:b + 1], in0=lnt[:, b:b + 1],
                                        scalar1=1.0 / D, scalar2=None, op0=ALU.mult)
                        e.tensor_scalar(out=lnt[:, b + 1:b + 2], in0=lnt[:, b + 1:b + 2],
                                        scalar1=1.0 / D, scalar2=None, op0=ALU.mult)
                        e.tensor_tensor(out=lnt[:, b + 2:b + 3], in0=lnt[:, b:b + 1],
                                        in1=lnt[:, b:b + 1], op=ALU.mult)
                        e.milestone(vv, 1)
                        e.tensor_tensor(out=lnt[:, b + 1:b + 2], in0=lnt[:, b + 1:b + 2],
                                        in1=lnt[:, b + 2:b + 3], op=ALU.subtract)
                        e.wait_ge(ss.h, ss0 + 4 * t + 2)
                        e.milestone(vv, 1)
                        e.scalar_tensor_tensor(out=lnt[:, b + 4:b + 5], in0=lnt[:, b:b + 1],
                                               scalar=-1.0, in1=lnt[:, b + 3:b + 4],
                                               op0=ALU.mult, op1=ALU.mult)
                        e.wait_ge(ss.h, ss0 + 4 * t + 3)
                        e.tensor_reduce(out=lnt[:, b + 5:b + 6], in_=xln[:],
                                        axis=AX.X, op=ALU.max, apply_absolute_value=True)
                        e.tensor_scalar(out=xs1[:, t:t + 1], in0=lnt[:, b + 5:b + 6],
                                        scalar1=1.0 / 127.0, scalar2=EPS,
                                        op0=ALU.mult, op1=ALU.add)
                        e.reciprocal(out=lnt[:, b + 5:b + 6], in_=xs1[:, t:t + 1])
                        e.wait_ge(ss.h, ss0 + 4 * t + 4)
                        e.scalar_tensor_tensor(out=i32s[:], in0=xln[:],
                                               scalar=lnt[:, b + 5:b + 6],
                                               in1=sgx[:],
                                               op0=ALU.mult, op1=ALU.add)
                        if t >= 2:
                            e.wait_ge(pe.h, pe0 + 2 * (t - 1))
                        e.milestone(vv, 1)
                        e.tensor_scalar(out=xq[t % 2][:], in0=i32s[:],
                                        scalar1=127.0, scalar2=-128.0,
                                        op0=ALU.min, op1=ALU.max)
                        for b2 in range(2):
                            e.wait_ge(pe.h, pe0 + 2 * t + b2 + 1)
                            view = xqT_t.ap().rearrange("p (k m) -> p k m", k=KD)
                            if b2 == 1:
                                e.milestone(vv, 1)
                            e.tensor_copy(
                                out=view[:, 8 * b2:8 * (b2 + 1), t * 128:(t + 1) * 128],
                                in_=psT[b2][:, :])

                @block.scalar
                def _(e):
                    for t in range(NT):
                        b = 6 * (t % 2)
                        ds, tk = tk_x[t]
                        e.wait_ge(ds.h, tk)
                        ss.inc(e.activation(out=junk[:], in_=xt[t % 2][:], func=AF.Square,
                                            accum_out=lnt[:, b + 1:b + 2]), 1)
                        e.wait_ge(vv.h, vv0 + 4 * t + 1)
                        e.activation(out=lnt[:, b + 3:b + 4], in_=lnt[:, b + 1:b + 2],
                                     func=AF.Ln, bias=epsb[:, 0:1], scale=1.0)
                        ss.inc(e.activation(out=lnt[:, b + 3:b + 4], in_=lnt[:, b + 3:b + 4],
                                            func=AF.Exp, scale=-0.5), 1)
                        e.wait_ge(vv.h, vv0 + 4 * t + 2)
                        ss.inc(e.activation(out=xln[:], in_=xt[t % 2][:], func=AF.Identity,
                                            scale=lnt[:, b + 3:b + 4],
                                            bias=lnt[:, b + 4:b + 5]), 1)
                        if t >= 1:
                            e.wait_ge(vv.h, vv0 + 4 * (t - 1) + 3)
                        e.sign(out=sgx[:], in_=xln[:])
                        ss.inc(e.mul(out=sgx[:], in_=sgx[:], mul=gsel[:, 0:1]), 1)

                @block.tensor
                def _(e):
                    e.wait_ge(syA.h, tk_const["identb"])
                    for t in range(NT):
                        e.wait_ge(vv.h, vv0 + 4 * t + 3)
                        for b2 in range(2):
                            if t >= 1:
                                e.wait_ge(vv.h, vv0 + 4 * (t - 1) + 4)
                            for j in range(8):
                                kc = 8 * b2 + j
                                ins = e.transpose(psT[b2][:, j * 128:(j + 1) * 128],
                                                  xq[t % 2][:, kc * 128:(kc + 1) * 128],
                                                  identb[:])
                            pe.inc(ins, 1)
                barrier()

        def emit_ffn(xqT_t, xqT_close, resid_src, dst, sub=0):
            with ExitStack() as p4:
                wgT = [p4.enter_context(nc.sbuf_tensor(U("wgT"), [128, 8192], BF16)) for i in range(2)]
                wuT = [p4.enter_context(nc.sbuf_tensor(U("wuT"), [128, 8192], BF16)) for i in range(2)]
                gbuf = [p4.enter_context(nc.sbuf_tensor(U("gb"), [128, 512], F32)) for i in range(2)]
                hbuf = [p4.enter_context(nc.sbuf_tensor(U("hb"), [128, 512], F32)) for i in range(2)]
                psg = [p4.enter_context(nc.psum_tensor(U("psg"), [128, 512], F32)) for i in range(2)]
                psu = [p4.enter_context(nc.psum_tensor(U("psu"), [128, 512], F32)) for i in range(2)]
                vv0, ss0, pe0 = vv.n, ss.n, pe.n
                NCH = GU_NCH
                sA0, sB0 = syA.n, syB.n
                gA0, gB0 = gpA.n, gpB.n

                def tk_spill(g):
                    return (gpA, gA0 + 16 * (g // 2 + 1)) if g % 2 == 0 else \
                           (gpB, gB0 + 16 * (g // 2 + 1))

                @block.vector
                def _(e):
                    e = V(e, vch, vstate)
                    e.memset(amaxh[:], 0.0)
                    for t in range(NT):
                        if t == NT - 1:
                            e.milestone(vv, 1)
                        e.tensor_tensor(out=cg1[:, t:t + 1], in0=xs1[:, t:t + 1],
                                        in1=wsb[:, 0:1], op=ALU.mult)

                @block.sync
                def _(e):
                    e.wait_ge(ccr["g"].h, 1)
                    e.wait_ge(ccr["u"].h, 1)
                    for j in range(NCH):
                        if j >= 2:
                            e.wait_ge(pe.h, pe0 + (j - 1) * NT)
                        blk = j // 2
                        co = (j % 2) * 512
                        for idx in range(8):
                            wT, wG = (wgT, wgT_g) if idx < 4 else (wuT, wuT_g)
                            kg = idx % 4
                            ds = syA if idx % 2 == 0 else syB
                            ds.go(e,
                                  wT[j % 2].ap().rearrange("p (k m) -> p k m", k=16)[:, 4 * kg:4 * kg + 4, :],
                                  wG[blk * D + kg * 512:blk * D + (kg + 1) * 512,
                                     co:co + 512].rearrange("(k p) m -> p k m", k=4))

                @block.tensor
                def _(e):
                    e.wait_ge(vv.h, vv0 + 1)
                    for j in range(NCH):
                        e.wait_ge(syA.h, sA0 + 64 * (j + 1))
                        e.wait_ge(syB.h, sB0 + 64 * (j + 1))
                        for t in range(NT):
                            g = j * NT + t
                            if g >= 2:
                                e.wait_ge(ss.h, ss0 + g - 1)
                                e.wait_ge(vv.h, vv0 + g)
                            for kc in range(KD):
                                lhsT = xqT_t[:, kc * TPC + t * 128:kc * TPC + (t + 1) * 128]
                                e.matmul(psg[g % 2][:], lhsT,
                                         wgT[j % 2][:, kc * 512:(kc + 1) * 512],
                                         start=(kc == 0), stop=(kc == KD - 1))
                                ins = e.matmul(psu[g % 2][:], lhsT,
                                               wuT[j % 2][:, kc * 512:(kc + 1) * 512],
                                               start=(kc == 0), stop=(kc == KD - 1))
                            pe.inc(ins, 1)

                @block.scalar
                def _(e):
                    e.wait_ge(vv.h, vv0 + 1)
                    for j in range(NCH):
                        for t in range(NT):
                            g = j * NT + t
                            e.wait_ge(pe.h, pe0 + g + 1)
                            if g >= 2:
                                e.wait_ge(vv.h, vv0 + g)
                            ss.inc(e.activation(out=gbuf[g % 2][:], in_=psg[g % 2][:],
                                                func=AF.Silu, scale=cg1[:, t:t + 1]), 1)

                @block.vector
                def _(e):
                    e = V(e, vch, vstate)
                    for j in range(NCH):
                        for t in range(NT):
                            g = j * NT + t
                            e.wait_ge(ss.h, ss0 + g + 1)
                            if g >= 2:
                                dsp, tkp = tk_spill(g - 2)
                                e.wait_ge(dsp.h, tkp)
                            e.tensor_tensor(out=hbuf[g % 2][:], in0=gbuf[g % 2][:],
                                            in1=psu[g % 2][:], op=ALU.mult)
                            e.tensor_reduce(out=lnt[:, 12:13], in_=hbuf[g % 2][:],
                                            axis=AX.X, op=ALU.max, apply_absolute_value=True)
                            e.milestone(vv, 1)
                            e.tensor_tensor(out=amaxh[:, t:t + 1], in0=amaxh[:, t:t + 1],
                                            in1=lnt[:, 12:13], op=ALU.max)

                @block.gpsimd
                def _(e):
                    for j in range(NCH):
                        for t in range(NT):
                            g = j * NT + t
                            e.wait_ge(vv.h, vv0 + 1 + g + 1)
                            ds = gpA if g % 2 == 0 else gpB
                            ds.go(e, hsp[t * 128:(t + 1) * 128, j * 512:(j + 1) * 512],
                                  hbuf[g % 2][:])
                barrier()
            xqT_close.close()
            if sub == 1:
                return

            for half in range(2):
                if sub in (3, 4) and half == 1:
                    continue
                t0 = half * 4
                ph = ExitStack()
                hqT = ph.enter_context(nc.sbuf_tensor(U("hqT"), [128, KI * 512], BF16))
                with ExitStack() as p5:
                    hload = [p5.enter_context(nc.sbuf_tensor(U("hl"), [128, 2048], F32)) for i in range(2)]
                    sgh = [p5.enter_context(nc.sbuf_tensor(U("sgh"), [128, 2048], F32)) for i in range(2)]
                    i32h = p5.enter_context(nc.sbuf_tensor(U("i32h"), [128, 2048], I32))
                    hqt = [p5.enter_context(nc.sbuf_tensor(U("hqt"), [128, 2048], BF16)) for i in range(2)]
                    psT5 = [p5.enter_context(nc.psum_tensor(U("psT5"), [128, 512], BF16)) for i in range(2)]
                    vv0, ss05, pe05 = vv.n, ss.n, pe.n
                    tk_hl = []
                    vvc = {}
                    _c = [0]

                    def _adv(key):
                        _c[0] += 1
                        vvc[key] = vv0 + _c[0]
                    for _i in range(16):
                        _adv(('s', _i))
                        _adv(('c', _i))
                        if _i >= 1:
                            for _b in range(4):
                                _adv(('e', _i - 1, _b))
                    for _b in range(4):
                        _adv(('e', 15, _b))

                    def mul_cnt(idx):
                        return ss05 + idx + 1

                    def pe_cnt(idx, b):
                        return pe05 + 4 * idx + b + 1

                    @block.sync
                    def _(e):
                        for idx in range(16):
                            if idx >= 2:
                                e.wait_ge(vv.h, vvc[('s', idx - 2)])
                                e.wait_ge(ss.h, mul_cnt(idx - 2))
                            t = t0 + idx // 4
                            cb = idx % 4
                            ds = syA if idx % 2 == 0 else syB
                            tk_hl.append((ds, ds.go(e, hload[idx % 2][:],
                                                    hsp[t * 128:(t + 1) * 128,
                                                        cb * 2048:(cb + 1) * 2048])))

                    @block.scalar
                    def _(e):
                        for idx in range(16):
                            ds, tk = tk_hl[idx]
                            e.wait_ge(ds.h, tk)
                            if idx >= 2:
                                e.wait_ge(vv.h, vvc[('s', idx - 2)])
                            e.sign(out=sgh[idx % 2][:], in_=hload[idx % 2][:])
                            ss.inc(e.mul(out=sgh[idx % 2][:], in_=sgh[idx % 2][:],
                                         mul=gsel[:, 0:1]), 1)

                    @block.vector
                    def _(e):
                        e = V(e, vch, vstate)
                        for i in range(4):
                            t = t0 + i
                            e.tensor_tensor(out=lnt[:, 12:13], in0=xs1[:, t:t + 1],
                                            in1=wsb[:, 1:2], op=ALU.mult)
                            e.tensor_tensor(out=lnt[:, 13:14], in0=amaxh[:, t:t + 1],
                                            in1=lnt[:, 12:13], op=ALU.mult)
                            e.tensor_scalar(out=sh1[:, t:t + 1], in0=lnt[:, 13:14],
                                            scalar1=1.0 / 127.0, scalar2=EPS,
                                            op0=ALU.mult, op1=ALU.add)
                            e.reciprocal(out=lnt[:, 13:14], in_=sh1[:, t:t + 1])
                            e.tensor_tensor(out=invq1[:, t:t + 1], in0=lnt[:, 12:13],
                                            in1=lnt[:, 13:14], op=ALU.mult)
                            e.scalar_tensor_tensor(out=shd1[:, t:t + 1], in0=sh1[:, t:t + 1],
                                                   scalar=0.5, in1=wsb[:, 2:3],
                                                   op0=ALU.mult, op1=ALU.mult)
                        hqTv = hqT.ap().rearrange("p (k m) -> p k m", k=KI)
                        for idx in range(16):
                            t = t0 + idx // 4
                            ds, tk = tk_hl[idx]
                            e.wait_ge(ds.h, tk)
                            e.wait_ge(ss.h, mul_cnt(idx))
                            e.milestone(vv, 1)
                            e.scalar_tensor_tensor(out=i32h[:], in0=hload[idx % 2][:],
                                                   scalar=invq1[:, t:t + 1],
                                                   in1=sgh[idx % 2][:],
                                                   op0=ALU.mult, op1=ALU.add)
                            if idx >= 2:
                                e.wait_ge(pe.h, pe_cnt(idx - 2, 3))
                            e.milestone(vv, 1)
                            e.tensor_scalar(out=hqt[idx % 2][:], in0=i32h[:],
                                            scalar1=127.0, scalar2=-128.0,
                                            op0=ALU.min, op1=ALU.max)
                            if idx >= 1:
                                j = idx - 1
                                i_j, cb_j = divmod(j, 4)
                                for b in range(4):
                                    e.wait_ge(pe.h, pe_cnt(j, b))
                                    kc0 = cb_j * 16 + 4 * b
                                    e.milestone(vv, 1)
                                    e.tensor_copy(
                                        out=hqTv[:, kc0:kc0 + 4,
                                                 i_j * 128:(i_j + 1) * 128],
                                        in_=psT5[b % 2][:, :])
                        j = 15
                        i_j, cb_j = divmod(j, 4)
                        for b in range(4):
                            e.wait_ge(pe.h, pe_cnt(j, b))
                            kc0 = cb_j * 16 + 4 * b
                            e.milestone(vv, 1)
                            e.tensor_copy(
                                out=hqTv[:, kc0:kc0 + 4,
                                         i_j * 128:(i_j + 1) * 128],
                                in_=psT5[b % 2][:, :])

                    @block.tensor
                    def _(e):
                        e.wait_ge(syA.h, tk_const["identb"])
                        for idx in range(16):
                            e.wait_ge(vv.h, vvc[('c', idx)])
                            for b in range(4):
                                gb = 4 * idx + b
                                if gb >= 2:
                                    g2 = gb - 2
                                    e.wait_ge(vv.h, vvc[('e', g2 // 4, g2 % 4)])
                                for tb in range(4):
                                    kc_l = 4 * b + tb
                                    ins = e.transpose(
                                        psT5[b % 2][:, tb * 128:(tb + 1) * 128],
                                        hqt[idx % 2][:, kc_l * 128:(kc_l + 1) * 128],
                                        identb[:])
                                pe.inc(ins, 1)
                barrier()
                if sub == 2:
                    ph.close()
                    continue

                if sub == 3:
                    with ExitStack() as pd:
                        dmp = [pd.enter_context(nc.sbuf_tensor(U("dmp"), [128, 2048], F32)) for i in range(2)]
                        vv0d = vv.n
                        gA0d, gB0d = gpA.n, gpB.n

                        @block.vector
                        def _(e):
                            e = V(e, vch, vstate)
                            for i in range(8):
                                if i >= 2:
                                    dso = gpA if i % 2 == 0 else gpB
                                    tko = (gA0d if i % 2 == 0 else gB0d) + 16 * ((i - 2) // 2 + 1)
                                    e.wait_ge(dso.h, tko)
                                e.milestone(vv, 1)
                                e.tensor_copy(out=dmp[i % 2][:],
                                              in_=hqT[:, i * 2048:(i + 1) * 2048])

                        @block.gpsimd
                        def _(e):
                            for i in range(8):
                                e.wait_ge(vv.h, vv0d + i + 1)
                                ds = gpA if i % 2 == 0 else gpB
                                ds.go(e, out_d[i * 128:(i + 1) * 128, :], dmp[i % 2][:])
                    barrier()
                    ph.close()
                    continue

                with ExitStack() as p6:
                    wdT = [p6.enter_context(nc.sbuf_tensor(U("wdT"), [128, KI * 256], BF16)) for i in range(2)]
                    fmb = [p6.enter_context(nc.sbuf_tensor(U("fmb"), [128, 512], F32)) for i in range(2)]
                    resb = [p6.enter_context(nc.sbuf_tensor(U("resb"), [128, D], F32)) for i in range(4)]
                    x1p = [p6.enter_context(nc.sbuf_tensor(U("x1p"), [128, 512], F32)) for i in range(2)]
                    outp = [p6.enter_context(nc.sbuf_tensor(U("outp"), [128, 512], F32)) for i in range(2)]
                    psfm = [p6.enter_context(nc.psum_tensor(U("psfm"), [128, 512], F32)) for i in range(2)]
                    ps2 = [p6.enter_context(nc.psum_tensor(U("ps2"), [128, 512], F32)) for i in range(4)]
                    vv0, ss0, pe0 = vv.n, ss.n, pe.n
                    gA0, gB0 = gpA.n, gpB.n
                    marks = {}
                    tk_wd = [None] * 8
                    tk_res = []

                    def tk_out(u):
                        return (gpA, gA0 + 16 * (u // 2 + 1)) if u % 2 == 0 else \
                               (gpB, gB0 + 16 * (u // 2 + 1))

                    def mm_cnt(g):
                        return pe0 + (1 if g == 0 else 2 * g)

                    def t_cnt(g):
                        return pe0 + (32 if g == 15 else 2 * g + 3)

                    @block.sync
                    def _(e):
                        for j in range(1 if sub == 4 else 8):
                            if j >= 2 and sub == 0:
                                e.wait_ge(pe.h, mm_cnt(2 * j - 3))
                            for kg in range(16):
                                ds = syA if kg % 2 == 0 else syB
                                ds.go(e,
                                      wdT[j % 2].ap().rearrange("p (k m) -> p k m", k=KI)[:, 4 * kg:4 * kg + 4, :],
                                      wdT_g[j * FI + kg * 512:j * FI + (kg + 1) * 512, :].rearrange(
                                          "(k p) m -> p k m", k=4))
                            tk_wd[j] = (syA.n, syB.n)

                    if sub == 4:
                        vv0d = vv.n
                        dsrc = wdT[0]

                        @block.vector
                        def _(e):
                            e = V(e, vch, vstate)
                            for dsq in (syA, syB, scA, scB):
                                e.wait_ge(dsq.h, dsq.n)
                            for i in range(8):
                                if i >= 4:
                                    dso, tko = tk_out(i - 4)
                                    e.wait_ge(dso.h, tko)
                                e.milestone(vv, 1)
                                e.tensor_copy(out=resb[i % 4][:],
                                              in_=dsrc[:, i * 2048:(i + 1) * 2048])

                        @block.gpsimd
                        def _(e):
                            for i in range(8):
                                e.wait_ge(vv.h, vv0d + i + 1)
                                ds = gpA if i % 2 == 0 else gpB
                                ds.go(e, out_d[i * 128:(i + 1) * 128, :], resb[i % 4][:])
                        barrier()
                        continue

                    @block.gpsimd
                    def _(e):
                        for i in range(4):
                            t = t0 + i
                            tk_res.append(gpC.go(e, resb[i][:],
                                                 resid_src[t * 128:(t + 1) * 128, :]))

                    dve_seq = []
                    for g in range(16):
                        dve_seq.append(('e', g))
                        if g >= 5 and (g - 5) % 4 == 0:
                            og = (g - 5) // 4
                            for ts in range(4):
                                dve_seq.append(('a', og * 4 + ts))
                    for ts in range(4):
                        dve_seq.append(('a', 12 + ts))
                    vv_of = {}
                    for i, item in enumerate(dve_seq):
                        vv_of[item] = vv0 + i + 1

                    @block.tensor
                    def _(e):
                        e.wait_ge(ccr["d"].h, 1)
                        e.wait_ge(syA.h, tk_const["identf"])
                        for g in range(16):
                            j, msub = divmod(g, 2)
                            if msub == 0:
                                e.wait_ge(syA.h, tk_wd[j][0])
                                e.wait_ge(syB.h, tk_wd[j][1])
                            if g >= 2:
                                e.wait_ge(vv.h, vv_of[('e', g - 2)])
                            for kc in range(KI):
                                ins = e.matmul(psfm[g % 2][:],
                                               wdT[j % 2][:, kc * 256 + msub * 128:kc * 256 + (msub + 1) * 128],
                                               hqT[:, kc * 512:(kc + 1) * 512],
                                               start=(kc == 0), stop=(kc == KI - 1))
                            pe.inc(ins, 1)
                            if g >= 1:
                                gT = g - 1
                                e.wait_ge(vv.h, vv_of[('e', gT)])
                                if gT >= 4:
                                    e.wait_ge(ss.h, ss0 + 4 * (gT // 4 - 1) + 4)
                                for ts in range(4):
                                    ins = e.transpose(ps2[ts][:, (gT % 4) * 128:(gT % 4 + 1) * 128],
                                                      fmb[gT % 2][:, ts * 128:(ts + 1) * 128],
                                                      identf[:])
                                pe.inc(ins, 1)
                        gT = 15
                        e.wait_ge(vv.h, vv_of[('e', gT)])
                        e.wait_ge(ss.h, ss0 + 4 * (gT // 4 - 1) + 4)
                        for ts in range(4):
                            ins = e.transpose(ps2[ts][:, (gT % 4) * 128:(gT % 4 + 1) * 128],
                                              fmb[gT % 2][:, ts * 128:(ts + 1) * 128],
                                              identf[:])
                        pe.inc(ins, 1)


                    @block.vector
                    def _(e):
                        e = V(e, vch, vstate)
                        for tkr in tk_res:
                            e.wait_ge(gpC.h, tkr)
                        for item in dve_seq:
                            kind, idx = item
                            if kind == 'e':
                                g = idx
                                e.wait_ge(pe.h, mm_cnt(g))
                                if g >= 2:
                                    e.wait_ge(pe.h, t_cnt(g - 2))
                                e.milestone(vv, 1)
                                e.tensor_copy(out=fmb[g % 2][:], in_=psfm[g % 2][:])
                            else:
                                u = idx
                                og, ts = divmod(u, 4)
                                e.wait_ge(ss.h, ss0 + u + 1)
                                if u >= 2:
                                    dso, tko = tk_out(u - 2)
                                    e.wait_ge(dso.h, tko)
                                e.milestone(vv, 1)
                                e.tensor_tensor(out=outp[u % 2][:],
                                                in0=x1p[u % 2][:],
                                                in1=resb[ts][:, og * 512:(og + 1) * 512],
                                                op=ALU.add)

                    @block.scalar
                    def _(e):
                        for og in range(4):
                            e.wait_ge(pe.h, t_cnt(4 * og + 3))
                            for ts in range(4):
                                t = t0 + ts
                                u = og * 4 + ts
                                if u >= 2:
                                    e.wait_ge(vv.h, vv_of[('a', u - 2)])
                                ss.inc(e.activation(out=x1p[u % 2][:], in_=ps2[ts][:],
                                                    func=AF.Identity,
                                                    scale=shd1[:, t:t + 1]), 1)

                    @block.gpsimd
                    def _(e):
                        for og in range(4):
                            for ts in range(4):
                                u = og * 4 + ts
                                t = t0 + ts
                                e.wait_ge(vv.h, vv_of[('a', u)])
                                ds = gpA if u % 2 == 0 else gpB
                                ds.go(e, dst[t * 128:(t + 1) * 128, og * 512:(og + 1) * 512],
                                      outp[u % 2][:])
                barrier()
                ph.close()

        if stage == 21:
            @block.gpsimd
            def _(e):
                gpC.go(e, out_d[0:128, 0:8], wsb[:])
                gpC.go(e, out_d[0:128, 8:16], invwsb[:])
                gpC.go(e, out_d[0:128, 16:18], gsel[:])
        if stage >= 12 or stage <= 3:
            xqT_scope = ExitStack()
            xqT1 = xqT_scope.enter_context(nc.sbuf_tensor("xqT1", [128, KD * TPC], BF16))
            if stage != 11 and stage != 21:
                emit_ln_quant(x_d, xqT1, "p3")
                if stage == 22:
                    @block.gpsimd
                    def _(e):
                        gpC.go(e, out_d[0:128, 0:8], xs1[:])
                        gpC.go(e, out_d[0:128, 16:144], xqT1[:, 0:128])
            if stage <= 3:
                emit_ffn(xqT1, xqT_scope, x_d, xres1)
            elif stage in (13, 14, 15, 16, 17, 18, 19, 20, 23, 24):
                emit_ffn(xqT1, xqT_scope, x_d, xres1,
                         sub={13: 1, 23: 3, 24: 4}.get(stage, 2))
            else:
                xqT_scope.close()

        if stage >= 11:
            @block.sync
            def _(e):
                if stage in (21, 22, 23, 24):
                    return
                if stage == 15:
                    syA.go(e, out_d, hsp[:, 0:D])
                elif stage == 17:
                    syA.go(e, out_d, hsp[:, FI - D:FI])
                else:
                    syA.go(e, out_d, x_d)
        elif stage >= 2:
            with ExitStack() as p:
                xt = [p.enter_context(nc.sbuf_tensor(U("l2x"), [128, D], F32)) for i in range(2)]
                xlnb = [p.enter_context(nc.sbuf_tensor(U("l2b"), [128, D], BF16)) for i in range(2)]
                stg = [p.enter_context(nc.sbuf_tensor(U("l2s"), [128, D], BF16)) for i in range(2)]
                junk = p.enter_context(nc.sbuf_tensor(U("l2jk"), [128, D], BF16))
                psT = [p.enter_context(nc.psum_tensor(U("l2p"), [128, 1024], BF16)) for i in range(2)]
                vv0, ss0, pe0 = vv.n, ss.n, pe.n
                gA0, gB0 = gpA.n, gpB.n
                tk_x = []

                def tk_stg(t):
                    return (gpA, gA0 + 16 * (t // 2 + 1)) if t % 2 == 0 else \
                           (gpB, gB0 + 16 * (t // 2 + 1))

                @block.sync
                def _(e):
                    for t in range(NT):
                        if t >= 2:
                            e.wait_ge(ss.h, ss0 + 3 * (t - 1))
                        ds = syA if t % 2 == 0 else syB
                        tk_x.append((ds, ds.go(e, xt[t % 2][:],
                                               xres1[t * 128:(t + 1) * 128, :])))

                @block.vector
                def _(e):
                    e = V(e, vch, vstate)
                    for t in range(NT):
                        b = 6 * (t % 2)
                        ds, tk = tk_x[t]
                        e.wait_ge(ds.h, tk)
                        e.tensor_reduce(out=lnt[:, b:b + 1], in_=xt[t % 2][:],
                                        axis=AX.X, op=ALU.add)
                        e.wait_ge(ss.h, ss0 + 3 * t + 1)
                        e.tensor_scalar(out=lnt[:, b:b + 1], in0=lnt[:, b:b + 1],
                                        scalar1=1.0 / D, scalar2=None, op0=ALU.mult)
                        e.tensor_scalar(out=lnt[:, b + 1:b + 2], in0=lnt[:, b + 1:b + 2],
                                        scalar1=1.0 / D, scalar2=None, op0=ALU.mult)
                        e.tensor_tensor(out=lnt[:, b + 2:b + 3], in0=lnt[:, b:b + 1],
                                        in1=lnt[:, b:b + 1], op=ALU.mult)
                        e.milestone(vv, 1)
                        e.tensor_tensor(out=lnt[:, b + 1:b + 2], in0=lnt[:, b + 1:b + 2],
                                        in1=lnt[:, b + 2:b + 3], op=ALU.subtract)
                        e.wait_ge(ss.h, ss0 + 3 * t + 2)
                        e.milestone(vv, 1)
                        e.scalar_tensor_tensor(out=lnt[:, b + 4:b + 5], in0=lnt[:, b:b + 1],
                                               scalar=-1.0, in1=lnt[:, b + 3:b + 4],
                                               op0=ALU.mult, op1=ALU.mult)
                        for b2 in range(2):
                            e.wait_ge(pe.h, pe0 + 2 * t + b2 + 1)
                            if t >= 2 and b2 == 0:
                                dsg, tkg = tk_stg(t - 2)
                                e.wait_ge(dsg.h, tkg)
                            view = stg[t % 2].ap().rearrange("p (k m) -> p k m", k=KD)
                            if b2 == 1:
                                e.milestone(vv, 1)
                            e.tensor_copy(out=view[:, 8 * b2:8 * (b2 + 1), :],
                                          in_=psT[b2][:, :])

                @block.scalar
                def _(e):
                    for t in range(NT):
                        b = 6 * (t % 2)
                        ds, tk = tk_x[t]
                        e.wait_ge(ds.h, tk)
                        ss.inc(e.activation(out=junk[:], in_=xt[t % 2][:], func=AF.Square,
                                            accum_out=lnt[:, b + 1:b + 2]), 1)
                        e.wait_ge(vv.h, vv0 + 3 * t + 1)
                        e.activation(out=lnt[:, b + 3:b + 4], in_=lnt[:, b + 1:b + 2],
                                     func=AF.Ln, bias=epsb[:, 0:1], scale=1.0)
                        ss.inc(e.activation(out=lnt[:, b + 3:b + 4], in_=lnt[:, b + 3:b + 4],
                                            func=AF.Exp, scale=-0.5), 1)
                        e.wait_ge(vv.h, vv0 + 3 * t + 2)
                        if t >= 2:
                            e.wait_ge(pe.h, pe0 + 2 * (t - 1))
                        ss.inc(e.activation(out=xlnb[t % 2][:], in_=xt[t % 2][:],
                                            func=AF.Identity,
                                            scale=lnt[:, b + 3:b + 4],
                                            bias=lnt[:, b + 4:b + 5]), 1)

                @block.tensor
                def _(e):
                    for t in range(NT):
                        e.wait_ge(ss.h, ss0 + 3 * t + 3)
                        for b2 in range(2):
                            if t >= 1:
                                e.wait_ge(vv.h, vv0 + 3 * (t - 1) + 3)
                            for j in range(8):
                                kc = 8 * b2 + j
                                ins = e.transpose(psT[b2][:, j * 128:(j + 1) * 128],
                                                  xlnb[t % 2][:, kc * 128:(kc + 1) * 128],
                                                  identb[:])
                            pe.inc(ins, 1)

                @block.gpsimd
                def _(e):
                    for t in range(NT):
                        e.wait_ge(vv.h, vv0 + 3 * t + 3)
                        ds = gpA if t % 2 == 0 else gpB
                        ds.go(e,
                              xln2T_in[:, t * 128:(t + 1) * 128].rearrange(
                                  "(k p) m -> p k m", k=KD),
                              stg[t % 2].ap().rearrange("p (k m) -> p k m", k=KD))
                    e.wait_ge(gpA.h, gpA.n)
                    e.wait_ge(gpB.h, gpB.n)
                    cc["g1"].inc(e.collective_compute(
                        "AllGather", ALU.bypass, replica_groups=G4,
                        ins=[xln2T_in], outs=[xln2T_g]), 1)
                    e.wait_ge(cc["g1"].h, 1)
                    ccr["g1"].inc(e.nop(), 1)
                barrier()

            qkv_scope = ExitStack()
            qf = qkv_scope.enter_context(nc.sbuf_tensor("qf", [128, NHC * SEQ], BF16))
            kf = qkv_scope.enter_context(nc.sbuf_tensor("kf", [128, NHC * SEQ], BF16))
            vt = qkv_scope.enter_context(nc.sbuf_tensor("vt", [128, (SEQ // 128) * 512], BF16))
            with ExitStack() as p8:
                wqT = p8.enter_context(nc.sbuf_tensor(U("wqT"), [128, KD * 512], BF16))
                wkT = p8.enter_context(nc.sbuf_tensor(U("wkT"), [128, KD * 512], BF16))
                wvT = p8.enter_context(nc.sbuf_tensor(U("wvT"), [128, KD * 512], BF16))
                xfm = [p8.enter_context(nc.sbuf_tensor(U("xfm"), [128, KD * 512], BF16)) for i in range(2)]
                psq = [p8.enter_context(nc.psum_tensor(U("psq"), [128, 512], F32)) for i in range(2)]
                vv0, pe0 = vv.n, pe.n
                marks8 = {}
                tk_xf = []

                @block.sync
                def _(e):
                    e.wait_ge(gpC.h, tk_cast["wkb"])
                    for kc in range(KD):
                        syA.go(e, wqT[:, kc * 512:(kc + 1) * 512],
                               wqb[:, kc * 128:(kc + 1) * 128], transpose=True)
                        syB.go(e, wkT[:, kc * 512:(kc + 1) * 512],
                               wkb[:, kc * 128:(kc + 1) * 128], transpose=True)
                    marks8["wqk"] = (syA.n, syB.n)

                @block.scalar
                def _(e):
                    e.wait_ge(gpC.h, tk_cast["wvb"])
                    for kc in range(KD):
                        scA.go(e, wvT[:, kc * 512:(kc + 1) * 512],
                               wvb[:, kc * 128:(kc + 1) * 128], transpose=True)
                    marks8["wv"] = (scA.n, scB.n)
                    e.wait_ge(ccr["g1"].h, 1)
                    for tc in range(8):
                        if tc >= 2:
                            e.wait_ge(pe.h, pe0 + 12 * (tc - 1))
                        blk = tc // 2
                        co = (tc % 2) * 512
                        for kg in range(4):
                            ds = scA if kg % 2 == 0 else scB
                            ds.go(e,
                                  xfm[tc % 2].ap().rearrange("p (k m) -> p k m", k=KD)[:, 4 * kg:4 * kg + 4, :],
                                  xln2T_g[blk * D + kg * 512:blk * D + (kg + 1) * 512,
                                          co:co + 512].rearrange("(k p) m -> p k m", k=4))
                        tk_xf.append((scA.n, scB.n))

                @block.tensor
                def _(e):
                    e.wait_ge(syA.h, marks8["wqk"][0])
                    e.wait_ge(syB.h, marks8["wqk"][1])
                    e.wait_ge(scA.h, marks8["wv"][0])
                    for tc in range(8):
                        e.wait_ge(scA.h, tk_xf[tc][0])
                        e.wait_ge(scB.h, tk_xf[tc][1])
                        for i in range(12):
                            gg = 12 * tc + i
                            if gg >= 2:
                                e.wait_ge(vv.h, vv0 + gg - 1)
                            for kc in range(KD):
                                if i < 4:
                                    ins = e.matmul(psq[gg % 2][:],
                                             wqT[:, kc * 512 + i * 128:kc * 512 + (i + 1) * 128],
                                             xfm[tc % 2][:, kc * 512:(kc + 1) * 512],
                                             start=(kc == 0), stop=(kc == KD - 1))
                                elif i < 8:
                                    m = i - 4
                                    ins = e.matmul(psq[gg % 2][:],
                                             wkT[:, kc * 512 + m * 128:kc * 512 + (m + 1) * 128],
                                             xfm[tc % 2][:, kc * 512:(kc + 1) * 512],
                                             start=(kc == 0), stop=(kc == KD - 1))
                                else:
                                    tsub = i - 8
                                    ins = e.matmul(psq[gg % 2][:],
                                             xfm[tc % 2][:, kc * 512 + tsub * 128:kc * 512 + (tsub + 1) * 128],
                                             wvT[:, kc * 512:(kc + 1) * 512],
                                             start=(kc == 0), stop=(kc == KD - 1))
                            pe.inc(ins, 1)

                @block.vector
                def _(e):
                    e = V(e, vch, vstate)
                    for tc in range(8):
                        for i in range(12):
                            gg = 12 * tc + i
                            e.wait_ge(pe.h, pe0 + gg + 1)
                            if i < 4:
                                dstv = qf[:, i * SEQ + tc * 512:i * SEQ + (tc + 1) * 512]
                            elif i < 8:
                                dstv = kf[:, (i - 4) * SEQ + tc * 512:(i - 4) * SEQ + (tc + 1) * 512]
                            else:
                                dstv = vt[:, (tc * 4 + i - 8) * 512:(tc * 4 + i - 7) * 512]
                            e.milestone(vv, 1)
                            e.tensor_copy(out=dstv, in_=psq[gg % 2][:])
                barrier()

            with ExitStack() as p9:
                Sb = [p9.enter_context(nc.sbuf_tensor(U("S"), [128, SEQ], F32)) for i in range(2)]
                Pb = [p9.enter_context(nc.sbuf_tensor(U("P"), [128, SEQ], BF16)) for i in range(2)]
                PT = p9.enter_context(nc.sbuf_tensor(U("PT"), [128, SEQ], BF16))
                aFh = [p9.enter_context(nc.sbuf_tensor(U("aFh"), [128, SEQ], BF16)) for i in range(2)]
                avsb = [p9.enter_context(nc.sbuf_tensor(U("avsb"), [128, 128], BF16)) for i in range(2)]
                pss = [p9.enter_context(nc.psum_tensor(U("pss"), [128, 512], F32)) for i in range(3)]
                pspt = [p9.enter_context(nc.psum_tensor(U("pspt"), [128, 1024], BF16)) for i in range(2)]
                psav = [p9.enter_context(nc.psum_tensor(U("psav"), [128, 512], F32)) for i in range(2)]
                pstf = p9.enter_context(nc.psum_tensor(U("pstf"), [128, 1024], BF16))

                units = [(h, p) for h in range(NHC) for p in range(32)]
                nun = len(units)
                chunks, ptb = [], []
                u_chunk_end, u_ptb_end = [], []
                for u, (h, p) in enumerate(units):
                    kv = 128 * (p + 1)
                    nch = (kv + 511) // 512
                    for ci in range(nch):
                        off = ci * 512
                        ln = min(512, kv - off)
                        chunks.append((u, ci, off, ln, 'd' if ci == nch - 1 else 's'))
                    u_chunk_end.append(len(chunks))
                    for bi in range(nch):
                        off = bi * 512
                        nblk = min(4, (kv - off + 127) // 128)
                        ptb.append((u, bi, off, nblk))
                    u_ptb_end.append(len(ptb))
                cntS, cntD = [], []
                cs = cd = 0
                for (_, _, _, _, eng) in chunks:
                    cs += (eng == 's')
                    cd += (eng == 'd')
                    cntS.append(cs)
                    cntD.append(cd)

                fsm, fes, fed = S("fsm"), S("fes"), S("fed")
                fm_, fex, fpt, fpe_ = S("fm"), S("fex"), S("fpt"), S("fpe")
                fav, fae, fat, fte, fli = S("fav"), S("fae"), S("fat"), S("fte"), S("fli")
                fsm0, fes0, fed0 = fsm.n, fes.n, fed.n
                fm0, fex0, fpt0, fpe0 = fm_.n, fex.n, fpt.n, fpe_.n
                fav0, fae0, fat0, fte0, fli0 = fav.n, fae.n, fat.n, fte.n, fli.n
                gA0, gB0 = gpA.n, gpB.n

                def tk_aF(h):
                    return (gpA, gA0 + 16 * (h // 2 + 1)) if h % 2 == 0 else \
                           (gpB, gB0 + 16 * (h // 2 + 1))

                @block.tensor
                def _(e):
                    def emit_S(u):
                        h, p = units[u]
                        c0 = u_chunk_end[u - 1] if u > 0 else 0
                        for g in range(c0, u_chunk_end[u]):
                            _, ci, off, ln, eng = chunks[g]
                            if g >= 3:
                                pg = g - 3
                                if chunks[pg][4] == 's':
                                    e.wait_ge(fes.h, fes0 + cntS[pg])
                                else:
                                    e.wait_ge(fed.h, fed0 + cntD[pg])
                            if ci == 0 and u >= 2:
                                e.wait_ge(fex.h, fex0 + u - 1)
                            fsm.inc(e.matmul(pss[g % 3][:, 0:ln],
                                     qf[:, h * SEQ + p * 128:h * SEQ + (p + 1) * 128],
                                     kf[:, h * SEQ + off:h * SEQ + off + ln],
                                     start=True, stop=True), 1)

                    emit_S(0)
                    for u in range(nun):
                        h, p = units[u]
                        kv = 128 * (p + 1)
                        b0 = u_ptb_end[u - 1] if u > 0 else 0
                        e.wait_ge(fex.h, fex0 + u + 1)
                        if u >= 1:
                            e.wait_ge(fav.h, fav0 + u)
                        for bb in range(b0, u_ptb_end[u]):
                            _, bi, off, nblk = ptb[bb]
                            if bb >= 2:
                                e.wait_ge(fpe_.h, fpe0 + bb - 1)
                            for j in range(nblk):
                                ins = e.transpose(pspt[bb % 2][:, j * 256:j * 256 + 128],
                                                  Pb[u % 2][:, off + j * 128:off + (j + 1) * 128],
                                                  identb[:])
                            fpt.inc(ins, 1)
                        e.wait_ge(fpe_.h, fpe0 + u_ptb_end[u])
                        if u >= 2:
                            e.wait_ge(fae.h, fae0 + u - 1)
                        for j in range(kv // 128):
                            ins = e.matmul(psav[u % 2][:, 0:128],
                                           PT[:, j * 128:(j + 1) * 128],
                                           vt[:, j * 512 + h * 128:j * 512 + (h + 1) * 128],
                                           start=(j == 0), stop=(j == kv // 128 - 1))
                        fav.inc(ins, 1)
                        if u + 1 < nun:
                            emit_S(u + 1)
                        e.wait_ge(fae.h, fae0 + u + 1)
                        if u >= 1:
                            e.wait_ge(fte.h, fte0 + u)
                        fat.inc(e.transpose(pstf[:, 0:128], avsb[u % 2][:], identb[:]), 1)

                @block.scalar
                def _(e):
                    for u in range(nun):
                        h, p = units[u]
                        kv = 128 * (p + 1)
                        c0 = u_chunk_end[u - 1] if u > 0 else 0
                        for g in range(c0, u_chunk_end[u]):
                            _, ci, off, ln, eng = chunks[g]
                            if eng != 's':
                                continue
                            e.wait_ge(fsm.h, fsm0 + g + 1)
                            fes.inc(e.activation(out=Sb[u % 2][:, off:off + ln],
                                         in_=pss[g % 3][:, 0:ln], func=AF.Identity), 1)
                        e.wait_ge(fm_.h, fm0 + u + 1)
                        if u >= 2:
                            e.wait_ge(fpt.h, fpt0 + u_ptb_end[u - 2])
                        pr = 4 * (u % 2)
                        fex.inc(e.activation(out=Pb[u % 2][:, 0:kv], in_=Sb[u % 2][:, 0:kv],
                                     func=AF.Exp, scale=SCALE,
                                     bias=fstat[:, pr + 1:pr + 2],
                                     accum_out=fstat[:, pr + 2:pr + 3]), 1)
                        e.wait_ge(fav.h, fav0 + u + 1)
                        e.wait_ge(fli.h, fli0 + u + 1)
                        if u >= 2:
                            e.wait_ge(fat.h, fat0 + u - 1)
                        fae.inc(e.activation(out=avsb[u % 2][:], in_=psav[u % 2][:, 0:128],
                                     func=AF.Identity, scale=fstat[:, pr + 3:pr + 4]), 1)

                @block.vector
                def _(e):
                    e = V(e, vch, vstate)
                    e.wait_ge(syA.h, tk_const["cmask"])
                    for u in range(nun):
                        h, p = units[u]
                        c0 = u_chunk_end[u - 1] if u > 0 else 0
                        for g in range(c0, u_chunk_end[u]):
                            _, ci, off, ln, eng = chunks[g]
                            if eng != 'd':
                                continue
                            e.wait_ge(fsm.h, fsm0 + g + 1)
                            e.milestone(fed, 1)
                            e.tensor_tensor(out=Sb[u % 2][:, off:off + ln],
                                            in0=pss[g % 3][:, 0:ln],
                                            in1=cmask[:, 512 - ln:512],
                                            op=ALU.add)
                        pr = 4 * (u % 2)
                        g_last = u_chunk_end[u] - 1
                        e.wait_ge(fes.h, fes0 + cntS[g_last])
                        kvu = 128 * (p + 1)
                        e.tensor_reduce(out=fstat[:, pr:pr + 1], in_=Sb[u % 2][:, 0:kvu],
                                        axis=AX.X, op=ALU.max)
                        e.milestone(fm_, 1)
                        e.tensor_scalar(out=fstat[:, pr + 1:pr + 2], in0=fstat[:, pr:pr + 1],
                                        scalar1=-SCALE, scalar2=None, op0=ALU.mult)
                        e.wait_ge(fex.h, fex0 + u + 1)
                        if u >= 2:
                            e.wait_ge(fae.h, fae0 + u - 1)
                        e.milestone(fli, 1)
                        e.reciprocal(out=fstat[:, pr + 3:pr + 4], in_=fstat[:, pr + 2:pr + 3])
                        b0 = u_ptb_end[u - 1] if u > 0 else 0
                        for bb in range(b0, u_ptb_end[u]):
                            _, bi, off, nblk = ptb[bb]
                            e.wait_ge(fpt.h, fpt0 + bb + 1)
                            view = PT.ap().rearrange("p (k m) -> p k m", k=32)
                            e.milestone(fpe_, 1)
                            e.tensor_copy(
                                out=view[:, 4 * bi:4 * bi + nblk, :],
                                in_=pspt[bb % 2].ap().rearrange("p (k m) -> p k m", k=4)[:, 0:nblk, 0:128])
                        e.wait_ge(fat.h, fat0 + u + 1)
                        if p == 0 and h >= 2:
                            dsa, tka = tk_aF(h - 2)
                            e.wait_ge(dsa.h, tka)
                        e.milestone(fte, 1)
                        e.tensor_copy(out=aFh[h % 2][:, p * 128:(p + 1) * 128],
                                      in_=pstf[:, 0:128])

                @block.gpsimd
                def _(e):
                    for h in range(NHC):
                        u = h * 32 + 31
                        e.wait_ge(fte.h, fte0 + u + 1)
                        ds = gpA if h % 2 == 0 else gpB
                        ds.go(e, attnF_d[h * 128:(h + 1) * 128, :], aFh[h % 2][:])
                barrier()
            qkv_scope.close()

            with ExitStack() as pA:
                attnW = pA.enter_context(nc.sbuf_tensor(U("attnW"), [128, NHC * SEQ], BF16))
                woTs = pA.enter_context(nc.sbuf_tensor(U("woTs"), [128, NHC * D], BF16))
                piece = [pA.enter_context(nc.sbuf_tensor(U("wop"), [128, D], BF16)) for i in range(2)]
                pswo = [pA.enter_context(nc.psum_tensor(U("pswo"), [128, 512], F32)) for i in range(8)]
                vv0, pe0 = vv.n, pe.n
                gA0, gB0 = gpA.n, gpB.n
                marksA = {}

                def tk_piece(t):
                    return (gpA, gA0 + 16 * (t // 2 + 1)) if t % 2 == 0 else \
                           (gpB, gB0 + 16 * (t // 2 + 1))

                @block.sync
                def _(e):
                    e.wait_ge(gpC.h, tk_cast["wob"])
                    for h in range(NHC):
                        syA.go(e, attnW[:, h * SEQ:(h + 1) * SEQ],
                               attnF_d[h * 128:(h + 1) * 128, :])
                    for kc in range(NHC):
                        syB.go(e, woTs[:, kc * D:(kc + 1) * D],
                               wob[:, kc * 128:(kc + 1) * 128], transpose=True)
                    marksA["ld"] = (syA.n, syB.n)

                @block.tensor
                def _(e):
                    e.wait_ge(syA.h, marksA["ld"][0])
                    e.wait_ge(syB.h, marksA["ld"][1])
                    for t in range(32):
                        if t >= 2:
                            e.wait_ge(vv.h, vv0 + 4 * (t - 1))
                        for oc in range(4):
                            for kc in range(NHC):
                                ins = e.matmul(pswo[4 * (t % 2) + oc][:],
                                         attnW[:, kc * SEQ + t * 128:kc * SEQ + (t + 1) * 128],
                                         woTs[:, kc * D + oc * 512:kc * D + (oc + 1) * 512],
                                         start=(kc == 0), stop=(kc == NHC - 1))
                        pe.inc(ins, 1)

                @block.vector
                def _(e):
                    e = V(e, vch, vstate)
                    for t in range(32):
                        e.wait_ge(pe.h, pe0 + t + 1)
                        if t >= 2:
                            dsp, tkp = tk_piece(t - 2)
                            e.wait_ge(dsp.h, tkp)
                        for oc in range(4):
                            if oc == 3:
                                e.milestone(vv, 4)
                            e.tensor_copy(out=piece[t % 2][:, oc * 512:(oc + 1) * 512],
                                          in_=pswo[4 * (t % 2) + oc][:])

                @block.gpsimd
                def _(e):
                    for t in range(32):
                        e.wait_ge(vv.h, vv0 + 4 * (t + 1))
                        ds = gpA if t % 2 == 0 else gpB
                        ds.go(e, wo_part[t * 128:(t + 1) * 128, :], piece[t % 2][:])
                    e.wait_ge(gpA.h, gpA.n)
                    e.wait_ge(gpB.h, gpB.n)
                    cc["g2"].inc(e.collective_compute(
                        "ReduceScatter", ALU.add, replica_groups=G4,
                        ins=[wo_part], outs=[wo_red]), 1)
                    e.wait_ge(cc["g2"].h, 1)
                    ccr["g2"].inc(e.nop(), 1)
                barrier()

            xqT2_scope = ExitStack()
            xqT2 = xqT2_scope.enter_context(nc.sbuf_tensor("xqT2", [128, KD * TPC], BF16))
            with ExitStack() as pB:
                x1t = [pB.enter_context(nc.sbuf_tensor(U("x1t"), [128, D], F32)) for i in range(2)]
                wrt = [pB.enter_context(nc.sbuf_tensor(U("wrt"), [128, D], BF16)) for i in range(2)]
                x2t = [pB.enter_context(nc.sbuf_tensor(U("x2t"), [128, D], F32)) for i in range(2)]
                xlnw = pB.enter_context(nc.sbuf_tensor(U("xlnw"), [128, D], F32))
                jkw = pB.enter_context(nc.sbuf_tensor(U("jkw"), [128, D], BF16))
                i32w = pB.enter_context(nc.sbuf_tensor(U("i32w"), [128, D], I32))
                xqw = [pB.enter_context(nc.sbuf_tensor(U("xqw"), [128, D], BF16)) for i in range(2)]
                sgxw = pB.enter_context(nc.sbuf_tensor(U("sgxw"), [128, D], F32))
                psT = [pB.enter_context(nc.psum_tensor(U("pBps"), [128, 1024], BF16)) for i in range(2)]
                vv0, ss0, pe0 = vv.n, ss.n, pe.n
                gA0, gB0 = gpA.n, gpB.n
                tk_x1 = []

                def tk_x2s(t):
                    return (gpA, gA0 + 16 * (t // 2 + 1)) if t % 2 == 0 else \
                           (gpB, gB0 + 16 * (t // 2 + 1))

                @block.sync
                def _(e):
                    e.wait_ge(ccr["g2"].h, 1)
                    for t in range(NT):
                        if t >= 2:
                            e.wait_ge(vv.h, vv0 + 5 * (t - 1))
                        ta = syA.go(e, x1t[t % 2][:], xres1[t * 128:(t + 1) * 128, :])
                        tb = syB.go(e, wrt[t % 2][:], wo_red[t * 128:(t + 1) * 128, :])
                        tk_x1.append((ta, tb))

                @block.vector
                def _(e):
                    e = V(e, vch, vstate)
                    for t in range(NT):
                        b = 6 * (t % 2)
                        ta, tb = tk_x1[t]
                        e.wait_ge(syA.h, ta)
                        e.wait_ge(syB.h, tb)
                        if t >= 2:
                            dsx, tkx = tk_x2s(t - 2)
                            e.wait_ge(dsx.h, tkx)
                            e.wait_ge(ss.h, ss0 + 4 * (t - 1))
                        e.milestone(vv, 1)
                        e.tensor_tensor(out=x2t[t % 2][:], in0=x1t[t % 2][:],
                                        in1=wrt[t % 2][:], op=ALU.add)
                        e.tensor_reduce(out=lnt[:, b:b + 1], in_=x2t[t % 2][:],
                                        axis=AX.X, op=ALU.add)
                        e.wait_ge(ss.h, ss0 + 4 * t + 1)
                        e.tensor_scalar(out=lnt[:, b:b + 1], in0=lnt[:, b:b + 1],
                                        scalar1=1.0 / D, scalar2=None, op0=ALU.mult)
                        e.tensor_scalar(out=lnt[:, b + 1:b + 2], in0=lnt[:, b + 1:b + 2],
                                        scalar1=1.0 / D, scalar2=None, op0=ALU.mult)
                        e.tensor_tensor(out=lnt[:, b + 2:b + 3], in0=lnt[:, b:b + 1],
                                        in1=lnt[:, b:b + 1], op=ALU.mult)
                        e.milestone(vv, 1)
                        e.tensor_tensor(out=lnt[:, b + 1:b + 2], in0=lnt[:, b + 1:b + 2],
                                        in1=lnt[:, b + 2:b + 3], op=ALU.subtract)
                        e.wait_ge(ss.h, ss0 + 4 * t + 2)
                        e.milestone(vv, 1)
                        e.scalar_tensor_tensor(out=lnt[:, b + 4:b + 5], in0=lnt[:, b:b + 1],
                                               scalar=-1.0, in1=lnt[:, b + 3:b + 4],
                                               op0=ALU.mult, op1=ALU.mult)
                        e.wait_ge(ss.h, ss0 + 4 * t + 3)
                        e.tensor_reduce(out=lnt[:, b + 5:b + 6], in_=xlnw[:],
                                        axis=AX.X, op=ALU.max, apply_absolute_value=True)
                        e.tensor_scalar(out=xs1[:, t:t + 1], in0=lnt[:, b + 5:b + 6],
                                        scalar1=1.0 / 127.0, scalar2=EPS,
                                        op0=ALU.mult, op1=ALU.add)
                        e.reciprocal(out=lnt[:, b + 5:b + 6], in_=xs1[:, t:t + 1])
                        e.wait_ge(ss.h, ss0 + 4 * t + 4)
                        e.scalar_tensor_tensor(out=i32w[:], in0=xlnw[:],
                                               scalar=lnt[:, b + 5:b + 6],
                                               in1=sgxw[:],
                                               op0=ALU.mult, op1=ALU.add)
                        if t >= 2:
                            e.wait_ge(pe.h, pe0 + 2 * (t - 1))
                        e.milestone(vv, 1)
                        e.tensor_scalar(out=xqw[t % 2][:], in0=i32w[:],
                                        scalar1=127.0, scalar2=-128.0,
                                        op0=ALU.min, op1=ALU.max)
                        for b2 in range(2):
                            e.wait_ge(pe.h, pe0 + 2 * t + b2 + 1)
                            view = xqT2.ap().rearrange("p (k m) -> p k m", k=KD)
                            if b2 == 1:
                                e.milestone(vv, 1)
                            e.tensor_copy(
                                out=view[:, 8 * b2:8 * (b2 + 1), t * 128:(t + 1) * 128],
                                in_=psT[b2][:, :])

                @block.scalar
                def _(e):
                    for t in range(NT):
                        b = 6 * (t % 2)
                        e.wait_ge(vv.h, vv0 + 5 * t + 1)
                        ss.inc(e.activation(out=jkw[:], in_=x2t[t % 2][:], func=AF.Square,
                                            accum_out=lnt[:, b + 1:b + 2]), 1)
                        e.wait_ge(vv.h, vv0 + 5 * t + 2)
                        e.activation(out=lnt[:, b + 3:b + 4], in_=lnt[:, b + 1:b + 2],
                                     func=AF.Ln, bias=epsb[:, 0:1], scale=1.0)
                        ss.inc(e.activation(out=lnt[:, b + 3:b + 4], in_=lnt[:, b + 3:b + 4],
                                            func=AF.Exp, scale=-0.5), 1)
                        e.wait_ge(vv.h, vv0 + 5 * t + 3)
                        ss.inc(e.activation(out=xlnw[:], in_=x2t[t % 2][:], func=AF.Identity,
                                            scale=lnt[:, b + 3:b + 4],
                                            bias=lnt[:, b + 4:b + 5]), 1)
                        if t >= 1:
                            e.wait_ge(vv.h, vv0 + 5 * (t - 1) + 4)
                        e.sign(out=sgxw[:], in_=xlnw[:])
                        ss.inc(e.mul(out=sgxw[:], in_=sgxw[:], mul=gsel[:, 0:1]), 1)

                @block.tensor
                def _(e):
                    for t in range(NT):
                        e.wait_ge(vv.h, vv0 + 5 * t + 4)
                        for b2 in range(2):
                            if t >= 1:
                                e.wait_ge(vv.h, vv0 + 5 * (t - 1) + 5)
                            for j in range(8):
                                kc = 8 * b2 + j
                                ins = e.transpose(psT[b2][:, j * 128:(j + 1) * 128],
                                                  xqw[t % 2][:, kc * 128:(kc + 1) * 128],
                                                  identb[:])
                            pe.inc(ins, 1)

                @block.gpsimd
                def _(e):
                    for t in range(NT):
                        e.wait_ge(vv.h, vv0 + 5 * t + 1)
                        ds = gpA if t % 2 == 0 else gpB
                        ds.go(e, xres2[t * 128:(t + 1) * 128, :], x2t[t % 2][:])
                barrier()

            if stage >= 3:
                emit_ffn(xqT2, xqT2_scope, xres2, out_d)
            else:
                xqT2_scope.close()

                @block.sync
                def _(e):
                    syA.go(e, out_d, xres2)
        else:
            @block.sync
            def _(e):
                syA.go(e, out_d, xres1)

        @block.sync
        def _(e):
            for ds in ALL_DS:
                e.wait_ge(ds.h, ds.n)

    return nc



def _host_constants():
    tri = np.tril(np.ones((128, 128), np.float32))
    cm = np.zeros((128, 512), np.float32)
    cm[:, 384:] = np.where(tri > 0, 0.0, NEG).astype(np.float32)
    identb = np.eye(128, dtype=BF)
    identf = np.eye(128, dtype=np.float32)
    return cm, identb, identf


def _in_maps(x, Wq, Wk, Wv, Wo, Wg, Wu, Wd):
    cm, identb, identf = _host_constants()
    maps = []
    for c in range(8):
        hs = 512 * (c % 4)
        maps.append(dict(
            x=np.ascontiguousarray(x[TPC * c:TPC * (c + 1)]),
            wq=np.ascontiguousarray(Wq[hs:hs + 512]),
            wk=np.ascontiguousarray(Wk[hs:hs + 512]),
            wv=np.ascontiguousarray(Wv[hs:hs + 512]),
            wo=np.ascontiguousarray(Wo[:, hs:hs + 512]),
            wg=np.ascontiguousarray(Wg[1024 * c:1024 * (c + 1)]),
            wu=np.ascontiguousarray(Wu[1024 * c:1024 * (c + 1)]),
            wd=np.ascontiguousarray(Wd[256 * c:256 * (c + 1)]),
            cmask=cm, identb=identb, identf=identf,
        ))
    return maps


def kernel(L0_attn_x, L0_attn_Wq, L0_attn_Wk, L0_attn_Wv, L0_attn_Wo,
           L0_ffn_w_gate, L0_ffn_w_up, L0_ffn_w_down):
    x = np.asarray(L0_attn_x, np.float32).reshape(8192, D)
    nc = _build(3)
    maps = _in_maps(x, np.asarray(L0_attn_Wq, np.float32),
                    np.asarray(L0_attn_Wk, np.float32),
                    np.asarray(L0_attn_Wv, np.float32),
                    np.asarray(L0_attn_Wo, np.float32),
                    np.asarray(L0_ffn_w_gate, np.float32),
                    np.asarray(L0_ffn_w_up, np.float32),
                    np.asarray(L0_ffn_w_down, np.float32))
    res = run_bass_kernel_spmd(nc, maps, core_ids=list(range(8)))
    out = np.concatenate([np.asarray(res.results[c]["out"]) for c in range(8)], axis=0)
    return out.reshape(2, 4096, D).astype(np.float32)
